# version 10
# baseline (speedup 1.0000x reference)
import contextlib
import numpy as np
import ml_dtypes
import concourse.bass as bass
import concourse.mybir as mybir
from concourse.bass_utils import run_bass_kernel_spmd

F32 = mybir.dt.float32
BF16 = mybir.dt.bfloat16
I32 = mybir.dt.int32
AF = mybir.ActivationFunctionType
ALU = mybir.AluOpType
AX = mybir.AxisListType

B, S, H = 2, 4096, 4096
NH, DN, DR, DV = 32, 128, 64, 128
QL, KVL = 1536, 512
II = 11008
SCALE = float(1.0 / np.sqrt(DN + DR))
EPS_RMS = 1e-6
EPS_LN = 1e-5
NKB = 8
NQT = 8
NIC = II // 128
NOC = H // 128
NKC = H // 128
NEG = -1e30

ENG_OWN = {'pe': 'tensor', 'dve': 'vector', 'act': 'scalar', 'pool': 'gpsimd'}


class Bld:
    def __init__(self, nc, es):
        self.nc = nc
        self.es = es
        self.cnt = {}
        self.sems = {}
        self.wm = {}
        self.res = {}

    def sem(self, name):
        if name not in self.sems:
            self.sems[name] = self.es.enter_context(self.nc.semaphore(name))
            self.cnt[name] = 0
        return self.sems[name]

    def inc(self, ins, name, n=1):
        ins.then_inc(self.sem(name), n)
        self.cnt[name] += n
        return (name, self.cnt[name])

    def tot(self, name):
        self.sem(name)
        return (name, self.cnt[name])

    def wait(self, eng, engname, ev):
        if ev is None:
            return
        name, c = ev
        if ENG_OWN.get(name) == engname:
            return
        wm = self.wm.setdefault(engname, {})
        if wm.get(name, 0) >= c:
            return
        eng.wait_ge(self.sems[name], c)
        wm[name] = c

    def rd(self, eng, engname, rname):
        r = self.res.get(rname)
        if r and r['w']:
            self.wait(eng, engname, r['w'])

    def wr(self, eng, engname, rname):
        r = self.res.get(rname)
        if r:
            if r['w']:
                self.wait(eng, engname, r['w'])
            for e in r['r']:
                self.wait(eng, engname, e)

    def note_w(self, rname, ev):
        self.res[rname] = {'w': ev, 'r': []}

    def note_r(self, rname, ev):
        r = self.res.setdefault(rname, {'w': None, 'r': []})
        r['r'].append(ev)

    def barrier(self):
        for engname in ('tensor', 'vector', 'scalar', 'gpsimd', 'sync'):
            wm = self.wm.setdefault(engname, {})
            for s in ('pe', 'dve', 'act', 'pool'):
                if s in self.cnt:
                    wm[s] = max(wm.get(s, 0), self.cnt[s])


def build_nc(dbg=()):
    nc = bass.Bass("TRN2", target_bir_lowering=False)

    d_xp = nc.dram_tensor("xp", [16, 128, NKC, 256], F32, kind="ExternalInput")
    d_xq = nc.dram_tensor("xq", [128, NKC, 1024], F32, kind="ExternalInput")
    d_cosk = nc.dram_tensor("cosk", [32, S], F32, kind="ExternalInput")
    d_sink = nc.dram_tensor("sink", [32, S], F32, kind="ExternalInput")
    d_cosq = nc.dram_tensor("cosq", [128, 1024], F32, kind="ExternalInput")
    d_sinq = nc.dram_tensor("sinq", [128, 1024], F32, kind="ExternalInput")
    d_mask = nc.dram_tensor("maskd", [128, 512], BF16, kind="ExternalInput")
    d_ident = nc.dram_tensor("ident", [128, 128], BF16, kind="ExternalInput")
    d_ones1 = nc.dram_tensor("ones1", [1, 128], F32, kind="ExternalInput")
    d_onesb = nc.dram_tensor("onesb", [128, 1], BF16, kind="ExternalInput")
    d_qnw = nc.dram_tensor("qnw", [128, 12], F32, kind="ExternalInput")
    d_kvnw = nc.dram_tensor("kvnw", [128, 4], F32, kind="ExternalInput")
    d_gs = nc.dram_tensor("gs", [128, 1], F32, kind="ExternalInput")
    d_us = nc.dram_tensor("us", [128, 1], F32, kind="ExternalInput")
    d_ds = nc.dram_tensor("ds", [128, 1], F32, kind="ExternalInput")
    d_eps = nc.dram_tensor("epsv", [1, 2], F32, kind="ExternalInput")
    d_wkva = nc.dram_tensor("wkva", [128, NKC, 576], F32, kind="ExternalInput")
    d_wqa = nc.dram_tensor("wqa", [6, 128, NKC, 256], F32, kind="ExternalInput")
    d_wqb = nc.dram_tensor("wqb", [8, 128, 12, 768], F32, kind="ExternalInput")
    d_wkbk = nc.dram_tensor("wkbk", [8, 128, 4, 512], F32, kind="ExternalInput")
    d_wkbv = nc.dram_tensor("wkbv", [8, 128, 4, 512], F32, kind="ExternalInput")
    d_wo = nc.dram_tensor("wo", [32, 128, 32, 128], F32, kind="ExternalInput")
    d_gw = nc.dram_tensor("gw", [NIC, 128, NKC, 128], I32, kind="ExternalInput")
    d_uw = nc.dram_tensor("uw", [NIC, 128, NKC, 128], I32, kind="ExternalInput")
    d_dw = nc.dram_tensor("dw", [8, 128, NIC, 512], I32, kind="ExternalInput")

    d_attn = nc.dram_tensor("attnh", [NH, 128, 1024], BF16,
                            kind="ExternalOutput" if 'attn' in dbg else "Internal")
    d_prod = nc.dram_tensor("prodh", [128, NIC, 1024], BF16,
                            kind="ExternalOutput" if 'prod' in dbg else "Internal")
    d_out = nc.dram_tensor("out", [NOC, 128, 1024], F32, kind="ExternalOutput")

    if 'h1q0' in dbg:
        d_dbg_sc = nc.dram_tensor("dbg_sc", [128, 512], F32, kind="ExternalOutput")
        d_dbg_pu1 = nc.dram_tensor("dbg_pu1", [128, 512], BF16, kind="ExternalOutput")
        d_dbg_vec = nc.dram_tensor("dbg_vec", [128, 32], F32, kind="ExternalOutput")
    d_dbg = {}
    for name, shape, dt in [
        ('kvn', [128, 4 * S], BF16), ('krot', [128, S], BF16),
        ('qan', [128, 12 * 1024], BF16), ('qn0', [128, 4 * 1024], BF16),
        ('pe0', [128, 2 * 1024], BF16), ('k0', [128, 2 * S], BF16),
        ('v0', [128, 32 * 256], BF16), ('pu0', [128, 8 * 512], BF16),
        ('h1n', [128, 32 * 1024], BF16), ('z2', [128, 32 * 1024], BF16),
    ]:
        if name in dbg:
            d_dbg[name] = nc.dram_tensor("dbg_" + name, shape, dt, kind="ExternalOutput")

    es0 = contextlib.ExitStack()
    with es0:
        b = Bld(nc, es0)
        c_qnw = es0.enter_context(nc.sbuf_tensor("c_qnw", [128, 12], F32))
        c_kvnw = es0.enter_context(nc.sbuf_tensor("c_kvnw", [128, 4], F32))
        c_gs = es0.enter_context(nc.sbuf_tensor("c_gs", [128, 1], F32))
        c_us = es0.enter_context(nc.sbuf_tensor("c_us", [128, 1], F32))
        c_ds = es0.enter_context(nc.sbuf_tensor("c_ds", [128, 1], F32))
        c_ones1 = es0.enter_context(nc.sbuf_tensor("c_ones1", [1, 128], F32))
        c_onesb = es0.enter_context(nc.sbuf_tensor("c_onesb", [128, 1], BF16))
        c_ident = es0.enter_context(nc.sbuf_tensor("c_ident", [128, 128], BF16))
        c_eps = es0.enter_context(nc.sbuf_tensor("c_eps", [1, 2], F32))
        ps = [es0.enter_context(nc.psum_tensor(f"ps{i}", [128, 512], F32)) for i in range(8)]
        psb = [p[:, :].bitcast(BF16) for p in ps]

        def dump(name, src_ap):
            if name in d_dbg:
                with nc.Block() as blk:
                    @blk.sync
                    def _(sy):
                        b.inc(sy.dma_start(d_dbg[name].ap(), src_ap), 'dbg', 16)
                        sy.wait_ge(b.sem('dbg'), b.cnt['dbg'])
                b.barrier()

        with nc.Block() as blk:
            @blk.sync
            def _(sy):
                for dst, src in [(c_qnw, d_qnw), (c_kvnw, d_kvnw), (c_gs, d_gs),
                                 (c_us, d_us), (c_ds, d_ds), (c_ones1, d_ones1),
                                 (c_onesb, d_onesb), (c_ident, d_ident),
                                 (c_eps, d_eps)]:
                    b.inc(sy.dma_start(dst[:, :], src.ap()), 'cs', 16)
                sy.wait_ge(b.sem('cs'), b.cnt['cs'])
        b.barrier()

        es_at = contextlib.ExitStack()
        with es_at:
            kvnT = es_at.enter_context(nc.sbuf_tensor("kvnT", [128, 4 * S], BF16))
            k_rot = es_at.enter_context(nc.sbuf_tensor("k_rot", [128, S], BF16))
            qanT = es_at.enter_context(nc.sbuf_tensor("qanT", [128, 12 * 1024], BF16))
            c_mask = es_at.enter_context(nc.sbuf_tensor("c_mask", [128, 512], BF16))
            c_cosq = es_at.enter_context(nc.sbuf_tensor("c_cosq", [128, 1024], F32))
            c_sinq = es_at.enter_context(nc.sbuf_tensor("c_sinq", [128, 1024], F32))

            with nc.Block() as blk:
                @blk.sync
                def _(sy):
                    for dst, src in [(c_mask, d_mask), (c_cosq, d_cosq), (c_sinq, d_sinq)]:
                        b.inc(sy.dma_start(dst[:, :], src.ap()), 'cs', 16)
                    sy.wait_ge(b.sem('cs'), b.cnt['cs'])
            b.barrier()

            with contextlib.ExitStack() as es1:
                wkva = es1.enter_context(nc.sbuf_tensor("s_wkva", [128, NKC * 576], BF16))
                xb = [es1.enter_context(nc.sbuf_tensor(f"xb{i}", [128, NKC * 256], BF16)) for i in range(2)]
                ckb = [es1.enter_context(nc.sbuf_tensor(f"ckb{i}", [32, 512], F32)) for i in range(2)]
                r2buf = es1.enter_context(nc.sbuf_tensor("r2buf", [32, S], BF16))
                tmp_a = es1.enter_context(nc.sbuf_tensor("tmp_a", [32, 256], F32))
                tmp_b = es1.enter_context(nc.sbuf_tensor("tmp_b", [32, 256], F32))
                sqst = [es1.enter_context(nc.sbuf_tensor(f"sqst{i}", [128, 4 * 256], BF16)) for i in range(2)]
                ssq_all = es1.enter_context(nc.sbuf_tensor("ssq_all", [1, S], F32))
                rstd_row = es1.enter_context(nc.sbuf_tensor("rstd_row", [1, S], F32))

                NT1 = 16
                TW = 256
                with nc.Block() as blk:
                    @blk.gpsimd
                    def _(g):
                        b.inc(g.dma_start(wkva[:, :],
                                          d_wkva.ap().rearrange("p c m -> p (c m)"),
                                          max_dma_last_dim=2048), 'wka', 16)

                    for t in range(NT1):
                        par = t % 2
                        w = slice(t * TW, (t + 1) * TW)

                        @blk.gpsimd
                        def _(g, t=t, par=par):
                            b.wr(g, 'gpsimd', f'xb{par}')
                            ev = b.inc(g.dma_start(xb[par][:, :],
                                                   d_xp.ap()[t].rearrange("p c m -> p (c m)"),
                                                   max_dma_last_dim=2048), f'xl{par}', 16)
                            b.note_w(f'xb{par}', ev)

                        @blk.sync
                        def _(sy, t=t, par=par, w=w):
                            b.wr(sy, 'sync', f'ck{par}')
                            b.inc(sy.dma_start(ckb[par][:, 0:TW], d_cosk.ap()[:, w]), f'ck{par}', 16)
                            ev = b.inc(sy.dma_start(ckb[par][:, TW:2 * TW], d_sink.ap()[:, w]), f'ck{par}', 16)
                            b.note_w(f'ck{par}', ev)

                        @blk.tensor
                        def _(tt, t=t, par=par):
                            b.wait(tt, 'tensor', ('wka', 16))
                            b.rd(tt, 'tensor', f'xb{par}')
                            ins = None
                            for m in range(4):
                                b.wr(tt, 'tensor', f'b{m}')
                                for c in range(NKC):
                                    ins = tt.matmul(ps[m][:, 0:TW],
                                                    wkva[:, c * 576 + m * 128: c * 576 + (m + 1) * 128],
                                                    xb[par][:, c * TW:(c + 1) * TW],
                                                    start=(c == 0), stop=(c == NKC - 1))
                            for bi, off in ((4, 512), (5, 544)):
                                b.wr(tt, 'tensor', f'b{bi}')
                                for c in range(NKC):
                                    ins = tt.matmul(ps[bi][0:32, 0:TW],
                                                    wkva[:, c * 576 + off: c * 576 + off + 32],
                                                    xb[par][:, c * TW:(c + 1) * TW],
                                                    start=(c == 0), stop=(c == NKC - 1))
                            ev = b.inc(ins, 'pe')
                            for m in range(6):
                                b.note_w(f'b{m}', ev)
                            b.note_r(f'xb{par}', ev)

                        @blk.vector
                        def _(v, t=t, par=par, w=w):
                            for m in range(6):
                                b.rd(v, 'vector', f'b{m}')
                            b.wr(v, 'vector', f'sqst{par}')
                            for m in range(4):
                                dst = kvnT[:, m * S + t * TW: m * S + (t + 1) * TW]
                                v.tensor_copy(dst, ps[m][:, 0:TW])
                                v.tensor_tensor(sqst[par][:, m * TW:(m + 1) * TW], dst, dst, ALU.mult)
                            b.rd(v, 'vector', f'ck{par}')
                            ck = ckb[par][:, 0:TW]
                            sk = ckb[par][:, TW:2 * TW]
                            v.tensor_tensor(tmp_a[:, :], ps[4][0:32, 0:TW], ck, ALU.mult)
                            v.tensor_tensor(tmp_b[:, :], ps[5][0:32, 0:TW], sk, ALU.mult)
                            v.tensor_tensor(k_rot[0:32, w], tmp_a[:, :], tmp_b[:, :], ALU.subtract)
                            v.tensor_tensor(tmp_a[:, :], ps[4][0:32, 0:TW], sk, ALU.mult)
                            v.tensor_tensor(tmp_b[:, :], ps[5][0:32, 0:TW], ck, ALU.mult)
                            ev = b.inc(v.tensor_tensor(r2buf[0:32, w], tmp_a[:, :], tmp_b[:, :], ALU.add), 'dve')
                            for m in range(6):
                                b.note_r(f'b{m}', ev)
                            b.note_r(f'ck{par}', ev)
                            b.note_w(f'sqst{par}', ev)

                        @blk.tensor
                        def _(tt, t=t, par=par):
                            b.wr(tt, 'tensor', 'b6')
                            b.rd(tt, 'tensor', f'sqst{par}')
                            ins = None
                            for m in range(4):
                                ins = tt.matmul(ps[6][0:1, 0:TW], c_onesb[:, 0:1],
                                                sqst[par][:, m * TW:(m + 1) * TW],
                                                start=(m == 0), stop=(m == 3))
                            ev = b.inc(ins, 'pe')
                            b.note_w('b6', ev)
                            b.note_r(f'sqst{par}', ev)

                        @blk.vector
                        def _(v, t=t, w=w):
                            b.rd(v, 'vector', 'b6')
                            ev = b.inc(v.tensor_copy(ssq_all[0:1, w], ps[6][0:1, 0:TW]), 'dve')
                            b.note_r('b6', ev)

                    @blk.sync
                    def _(sy):
                        sy.wait_ge(b.sem('dve'), b.cnt['dve'])
                        b.inc(sy.dma_start(k_rot[64:96, :], k_rot[0:32, :]), 'krm', 16)
                        b.inc(sy.dma_start(k_rot[32:64, :], r2buf[0:32, :]), 'krm', 16)
                        b.inc(sy.dma_start(k_rot[96:128, :], r2buf[0:32, :]), 'krm', 16)
                        sy.wait_ge(b.sem('krm'), b.cnt['krm'])

                    @blk.scalar
                    def _(s):
                        b.wait(s, 'scalar', b.tot('dve'))
                        ev = b.inc(s.activation(rstd_row[:, :], ssq_all[:, :], AF.Sqrt,
                                                bias=c_eps[0:1, 0:1], scale=float(1.0 / KVL)), 'act')
                        b.note_w('rstd1', ev)

                    @blk.vector
                    def _(v):
                        b.rd(v, 'vector', 'rstd1')
                        ev = b.inc(v.reciprocal(rstd_row[:, :], rstd_row[:, :]), 'dve')
                        b.note_w('rstd1', ev)

                    for w8 in range(8):
                        wsl = slice(w8 * 512, (w8 + 1) * 512)

                        @blk.tensor
                        def _(tt, w8=w8, wsl=wsl):
                            b.wr(tt, 'tensor', 'b7')
                            b.rd(tt, 'tensor', 'rstd1')
                            ev = b.inc(tt.matmul(ps[7][:, 0:512], c_ones1[0:1, :],
                                                 rstd_row[0:1, wsl], start=True, stop=True), 'pe')
                            b.note_w('b7', ev)

                        @blk.vector
                        def _(v, w8=w8):
                            b.rd(v, 'vector', 'b7')
                            ins = None
                            for m in range(4):
                                sl = kvnT[:, m * S + w8 * 512: m * S + (w8 + 1) * 512]
                                v.tensor_tensor(sl, sl, ps[7][:, 0:512], ALU.mult)
                                ins = v.tensor_scalar(sl, sl, c_kvnw[:, m:m + 1], None, ALU.mult)
                            ev = b.inc(ins, 'dve')
                            b.note_r('b7', ev)
                b.barrier()

            dump('kvn', kvnT[:, :])
            dump('krot', k_rot[:, :])

            with contextlib.ExitStack() as es2:
                xqb = es2.enter_context(nc.sbuf_tensor("xqb", [128, NKC * 512], BF16))
                wqab = [es2.enter_context(nc.sbuf_tensor(f"wqab{i}", [128, NKC * 256], BF16)) for i in range(2)]
                sq_q = [es2.enter_context(nc.sbuf_tensor(f"sq_q{i}", [128, 512], BF16)) for i in range(2)]
                ssq_q = es2.enter_context(nc.sbuf_tensor("ssq_q", [1, 1024], F32))
                rstd_q = es2.enter_context(nc.sbuf_tensor("rstd_q", [1, 1024], F32))

                with nc.Block() as blk:
                    for th in range(2):
                        @blk.gpsimd
                        def _(g, th=th):
                            b.wr(g, 'gpsimd', 'xqb')
                            ev = b.inc(g.dma_start(
                                xqb[:, :].rearrange("p (c m) -> p c m", c=NKC),
                                d_xq.ap()[:, :, th * 512:(th + 1) * 512],
                                max_dma_last_dim=2048), 'xq', 16)
                            b.note_w('xqb', ev)

                        for mg in range(6):
                            par = mg % 2

                            @blk.gpsimd
                            def _(g, th=th, mg=mg, par=par):
                                b.wr(g, 'gpsimd', f'wqab{par}')
                                ev = b.inc(g.dma_start(wqab[par][:, :],
                                                       d_wqa.ap()[mg].rearrange("p c m -> p (c m)"),
                                                       max_dma_last_dim=2048), f'wqa{par}', 16)
                                b.note_w(f'wqab{par}', ev)

                            for mm in range(2):
                                mc = mg * 2 + mm
                                bank = mc % 6

                                @blk.tensor
                                def _(tt, th=th, mm=mm, mc=mc, bank=bank, par=par):
                                    b.rd(tt, 'tensor', 'xqb')
                                    b.rd(tt, 'tensor', f'wqab{par}')
                                    b.wr(tt, 'tensor', f'b{bank}')
                                    ins = None
                                    for c in range(NKC):
                                        ins = tt.matmul(ps[bank][:, 0:512],
                                                        wqab[par][:, c * 256 + mm * 128: c * 256 + (mm + 1) * 128],
                                                        xqb[:, c * 512:(c + 1) * 512],
                                                        start=(c == 0), stop=(c == NKC - 1))
                                    ev = b.inc(ins, 'pe')
                                    b.note_w(f'b{bank}', ev)
                                    b.note_r('xqb', ev)
                                    b.note_r(f'wqab{par}', ev)

                                @blk.vector
                                def _(v, th=th, mc=mc, bank=bank):
                                    spar = mc % 2
                                    b.rd(v, 'vector', f'b{bank}')
                                    b.wr(v, 'vector', f'sq_q{spar}')
                                    dst = qanT[:, mc * 1024 + th * 512: mc * 1024 + (th + 1) * 512]
                                    v.tensor_copy(dst, ps[bank][:, 0:512])
                                    ev = b.inc(v.tensor_tensor(sq_q[spar][:, :], dst, dst, ALU.mult), 'dve')
                                    b.note_r(f'b{bank}', ev)
                                    b.note_w(f'sq_q{spar}', ev)

                                @blk.tensor
                                def _(tt, th=th, mc=mc):
                                    spar = mc % 2
                                    if mc == 0:
                                        b.wr(tt, 'tensor', 'b6')
                                    b.rd(tt, 'tensor', f'sq_q{spar}')
                                    ins = tt.matmul(ps[6][0:1, 0:512], c_onesb[:, 0:1], sq_q[spar][:, :],
                                                    start=(mc == 0), stop=(mc == 11))
                                    ev = b.inc(ins, 'pe')
                                    b.note_r(f'sq_q{spar}', ev)
                                    if mc == 11:
                                        b.note_w('b6', ev)

                        @blk.vector
                        def _(v, th=th):
                            b.rd(v, 'vector', 'b6')
                            ev = b.inc(v.tensor_copy(ssq_q[0:1, th * 512:(th + 1) * 512],
                                                     ps[6][0:1, 0:512]), 'dve')
                            b.note_r('b6', ev)

                    @blk.scalar
                    def _(s):
                        b.wait(s, 'scalar', b.tot('dve'))
                        ev = b.inc(s.activation(rstd_q[:, :], ssq_q[:, :], AF.Sqrt,
                                                bias=c_eps[0:1, 0:1], scale=float(1.0 / QL)), 'act')
                        b.note_w('rstdq', ev)

                    @blk.vector
                    def _(v):
                        b.rd(v, 'vector', 'rstdq')
                        ev = b.inc(v.reciprocal(rstd_q[:, :], rstd_q[:, :]), 'dve')
                        b.note_w('rstdq', ev)

                    for th in range(2):
                        @blk.tensor
                        def _(tt, th=th):
                            b.wr(tt, 'tensor', 'b7')
                            b.rd(tt, 'tensor', 'rstdq')
                            ev = b.inc(tt.matmul(ps[7][:, 0:512], c_ones1[0:1, :],
                                                 rstd_q[0:1, th * 512:(th + 1) * 512],
                                                 start=True, stop=True), 'pe')
                            b.note_w('b7', ev)

                        @blk.vector
                        def _(v, th=th):
                            b.rd(v, 'vector', 'b7')
                            ins = None
                            for mc in range(12):
                                sl = qanT[:, mc * 1024 + th * 512: mc * 1024 + (th + 1) * 512]
                                v.tensor_tensor(sl, sl, ps[7][:, 0:512], ALU.mult)
                                ins = v.tensor_scalar(sl, sl, c_qnw[:, mc:mc + 1], None, ALU.mult)
                            ev = b.inc(ins, 'dve')
                            b.note_r('b7', ev)
                b.barrier()

            dump('qan', qanT[:, :])

            with contextlib.ExitStack() as es3:
                wqbb = es3.enter_context(nc.sbuf_tensor("wqbb", [128, 12 * 768], BF16))
                wkbkb = es3.enter_context(nc.sbuf_tensor("wkbkb", [128, 4 * 512], BF16))
                wkbvb = es3.enter_context(nc.sbuf_tensor("wkbvb", [128, 4 * 512], BF16))
                qn_grp = es3.enter_context(nc.sbuf_tensor("qn_grp", [128, 4 * 1024], BF16))
                peA = es3.enter_context(nc.sbuf_tensor("peA", [128, 1024], BF16))
                peB = es3.enter_context(nc.sbuf_tensor("peB", [128, 1024], BF16))
                r1b = es3.enter_context(nc.sbuf_tensor("r1b", [128, 1024], BF16))
                r2b = es3.enter_context(nc.sbuf_tensor("r2b", [128, 1024], BF16))
                tq1 = es3.enter_context(nc.sbuf_tensor("tq1", [128, 512], F32))
                tq2 = es3.enter_context(nc.sbuf_tensor("tq2", [128, 512], F32))
                k_sub = es3.enter_context(nc.sbuf_tensor("k_sub", [128, 2 * S], BF16))
                v_sub = es3.enter_context(nc.sbuf_tensor("v_sub", [128, 32 * 258], BF16))
                probs_u = [es3.enter_context(nc.sbuf_tensor(f"probs_u{i}", [128, 8 * 512], BF16)) for i in range(2)]
                probsT = [es3.enter_context(nc.sbuf_tensor(f"probsT{i}", [128, S], BF16)) for i in range(2)]
                outst = [es3.enter_context(nc.sbuf_tensor(f"outst{i}", [128, 8 * 128], BF16)) for i in range(2)]
                arow = [es3.enter_context(nc.sbuf_tensor(f"arow{i}", [128, 1024], BF16)) for i in range(2)]
                mxb = [es3.enter_context(nc.sbuf_tensor(f"mxb{i}", [128, 8], F32)) for i in range(2)]
                biasv = [es3.enter_context(nc.sbuf_tensor(f"biasv{i}", [128, 1], F32)) for i in range(2)]
                ssumb = [es3.enter_context(nc.sbuf_tensor(f"ssumb{i}", [128, 1], F32)) for i in range(2)]
                recipb = [es3.enter_context(nc.sbuf_tensor(f"recipb{i}", [128, 1], F32)) for i in range(2)]
                tred = [es3.enter_context(nc.sbuf_tensor(f"tred{i}", [128, 16], F32)) for i in range(2)]
                dbsc = es3.enter_context(nc.sbuf_tensor("dbsc", [128, 512], F32)) if 'h1q0' in dbg else None
                dbvec = es3.enter_context(nc.sbuf_tensor("dbvec", [128, 32], F32)) if 'h1q0' in dbg else None
                dbpu = es3.enter_context(nc.sbuf_tensor("dbpu", [128, 512], BF16)) if 'h1q0' in dbg else None
                def _is_tgt(g4, sub, j2, qt):
                    return 'h1q0' in dbg and g4 == 0 and sub == 0 and j2 == 1 and qt == 0

                with nc.Block() as blk:
                    @blk.vector
                    def _(v):
                        for sc in range(32):
                            v.memset(v_sub[:, sc * 258 + 128: sc * 258 + 129], 1.0)
                            v.memset(v_sub[:, sc * 258 + 257: sc * 258 + 258], 1.0)
                        b.inc(nc.vector.memset(v_sub[:, 0:1], 0.0), 'dve')

                    for g4 in range(8):
                        @blk.gpsimd
                        def _(g, g4=g4):
                            b.wr(g, 'gpsimd', 'wqbb')
                            ev = b.inc(g.dma_start(wqbb[:, :],
                                                   d_wqb.ap()[g4].rearrange("p c m -> p (c m)"),
                                                   max_dma_last_dim=2048), 'wqb', 16)
                            b.note_w('wqbb', ev)
                            b.wr(g, 'gpsimd', 'wkbkb')
                            ev = b.inc(g.dma_start(wkbkb[:, :],
                                                   d_wkbk.ap()[g4].rearrange("p c m -> p (c m)"),
                                                   max_dma_last_dim=2048), 'wkk', 16)
                            b.note_w('wkbkb', ev)
                            b.wr(g, 'gpsimd', 'wkbvb')
                            ev = b.inc(g.dma_start(wkbvb[:, :],
                                                   d_wkbv.ap()[g4].rearrange("p c m -> p (c m)"),
                                                   max_dma_last_dim=2048), 'wkv', 16)
                            b.note_w('wkbvb', ev)

                        for mc in range(6):
                            for qb in range(2):
                                bank = (mc * 2 + qb) % 8 if mc < 4 else (0 if mc == 4 else 2) + qb

                                @blk.tensor
                                def _(tt, g4=g4, mc=mc, qb=qb, bank=bank):
                                    b.rd(tt, 'tensor', 'wqbb')
                                    b.wr(tt, 'tensor', f'b{bank}')
                                    ins = None
                                    for k in range(12):
                                        ins = tt.matmul(
                                            ps[bank][:, 0:512],
                                            wqbb[:, k * 768 + mc * 128: k * 768 + (mc + 1) * 128],
                                            qanT[:, k * 1024 + qb * 512: k * 1024 + (qb + 1) * 512],
                                            start=(k == 0), stop=(k == 11))
                                    ev = b.inc(ins, 'pe')
                                    b.note_w(f'b{bank}', ev)
                                    b.note_r('wqbb', ev)

                                if mc < 4:
                                    @blk.scalar
                                    def _(s, g4=g4, mc=mc, qb=qb, bank=bank):
                                        b.rd(s, 'scalar', f'b{bank}')
                                        b.wr(s, 'scalar', 'qn_grp')
                                        ev = b.inc(s.copy(
                                            qn_grp[:, mc * 1024 + qb * 512: mc * 1024 + (qb + 1) * 512],
                                            ps[bank][:, 0:512]), 'act')
                                        b.note_r(f'b{bank}', ev)
                                        if mc == 3 and qb == 1:
                                            b.note_w('qn_grp', ev)

                        for qb in range(2):
                            @blk.vector
                            def _(v, g4=g4, qb=qb):
                                for bank in (0 + qb, 2 + qb):
                                    b.rd(v, 'vector', f'b{bank}')
                                b.wr(v, 'vector', 'r1b')
                                b.wr(v, 'vector', 'r2b')
                                wsl = slice(qb * 512, (qb + 1) * 512)
                                cq = c_cosq[:, wsl]
                                sq = c_sinq[:, wsl]
                                x1 = ps[0 + qb][:, 0:512]
                                x2 = ps[2 + qb][:, 0:512]
                                v.tensor_tensor(tq1[:, :], x1, cq, ALU.mult)
                                v.tensor_tensor(tq2[:, :], x2, sq, ALU.mult)
                                v.tensor_tensor(r1b[:, wsl], tq1[:, :], tq2[:, :], ALU.subtract)
                                v.tensor_tensor(tq1[:, :], x1, sq, ALU.mult)
                                v.tensor_tensor(tq2[:, :], x2, cq, ALU.mult)
                                ev = b.inc(v.tensor_tensor(r2b[:, wsl], tq1[:, :], tq2[:, :], ALU.add), 'dve')
                                for bank in (0 + qb, 2 + qb):
                                    b.note_r(f'b{bank}', ev)
                                if qb == 1:
                                    b.note_w('r1b', ev)
                                    b.note_w('r2b', ev)

                        @blk.sync
                        def _(sy, g4=g4):
                            b.rd(sy, 'sync', 'r1b')
                            b.rd(sy, 'sync', 'r2b')
                            b.wr(sy, 'sync', 'peA')
                            b.wr(sy, 'sync', 'peB')
                            ev = None
                            for h4 in range(4):
                                src_r1 = r1b[h4 * 32:(h4 + 1) * 32, :]
                                src_r2 = r2b[h4 * 32:(h4 + 1) * 32, :]
                                dstT = peA if h4 < 2 else peB
                                pb = 64 * (h4 % 2)
                                b.inc(sy.dma_start(dstT[pb:pb + 32, :], src_r1), 'pem', 16)
                                ev = b.inc(sy.dma_start(dstT[pb + 32:pb + 64, :], src_r2), 'pem', 16)
                            b.note_w('peA', ev)
                            b.note_w('peB', ev)
                            b.note_r('r1b', ev)
                            b.note_r('r2b', ev)

                        for sub in range(2):
                            for j2 in range(2):
                                for kb in range(NKB):
                                    bank = (j2 * NKB + kb) % 8

                                    @blk.tensor
                                    def _(tt, g4=g4, sub=sub, j2=j2, kb=kb, bank=bank):
                                        b.rd(tt, 'tensor', 'wkbkb')
                                        b.wr(tt, 'tensor', f'b{bank}')
                                        h4 = 2 * sub + j2
                                        ins = None
                                        for c in range(4):
                                            ins = tt.matmul(
                                                ps[bank][:, 0:512],
                                                wkbkb[:, c * 512 + h4 * 128: c * 512 + (h4 + 1) * 128],
                                                kvnT[:, c * S + kb * 512: c * S + (kb + 1) * 512],
                                                start=(c == 0), stop=(c == 3))
                                        ev = b.inc(ins, 'pe')
                                        b.note_w(f'b{bank}', ev)
                                        if j2 == 1 and kb == NKB - 1:
                                            b.note_r('wkbkb', ev)

                                    @blk.scalar
                                    def _(s, j2=j2, kb=kb, bank=bank):
                                        b.rd(s, 'scalar', f'b{bank}')
                                        b.wr(s, 'scalar', 'k_sub')
                                        ev = b.inc(s.copy(
                                            k_sub[:, j2 * S + kb * 512: j2 * S + (kb + 1) * 512],
                                            ps[bank][:, 0:512]), 'act')
                                        b.note_r(f'b{bank}', ev)
                                        if j2 == 1 and kb == NKB - 1:
                                            b.note_w('k_sub', ev)

                            for sc in range(32):
                                bank = sc % 8

                                @blk.tensor
                                def _(tt, sub=sub, sc=sc, bank=bank):
                                    b.rd(tt, 'tensor', 'wkbvb')
                                    b.wr(tt, 'tensor', f'b{bank}')
                                    ins = None
                                    for c in range(4):
                                        ins = tt.matmul(
                                            ps[bank][:, 0:256],
                                            kvnT[:, c * S + sc * 128: c * S + (sc + 1) * 128],
                                            wkbvb[:, c * 512 + sub * 256: c * 512 + (sub + 1) * 256],
                                            start=(c == 0), stop=(c == 3))
                                    ev = b.inc(ins, 'pe')
                                    b.note_w(f'b{bank}', ev)
                                    if sc == 31:
                                        b.note_r('wkbvb', ev)

                                @blk.vector
                                def _(v, sc=sc, bank=bank):
                                    b.rd(v, 'vector', f'b{bank}')
                                    b.wr(v, 'vector', 'v_sub')
                                    v.tensor_copy(v_sub[:, sc * 258: sc * 258 + 128], ps[bank][:, 0:128])
                                    ev = b.inc(v.tensor_copy(v_sub[:, sc * 258 + 129: sc * 258 + 257],
                                                             ps[bank][:, 128:256]), 'dve')
                                    b.note_r(f'b{bank}', ev)
                                    if sc == 31:
                                        b.note_w('v_sub', ev)

                            P = []
                            for j2 in range(2):
                                h4 = 2 * sub + j2
                                P.append(dict(
                                    j2=j2, h4=h4, hh=4 * g4 + h4,
                                    peT=peA if h4 < 2 else peB,
                                    peName='peA' if h4 < 2 else 'peB',
                                    pb=64 * (h4 % 2), apar=(4 * g4 + h4) % 2,
                                    pvb=0 if j2 == 0 else 7,
                                    hdb=1 if j2 == 0 else 6,
                                    BK=(lambda kb, j2=j2: kb if j2 == 0 else 7 - kb)))

                            def emit_scores(p, qt):
                                @blk.tensor
                                def _(tt, p=p, qt=qt):
                                    j2, h4, BK = p['j2'], p['h4'], p['BK']
                                    b.rd(tt, 'tensor', 'qn_grp')
                                    b.rd(tt, 'tensor', p['peName'])
                                    b.rd(tt, 'tensor', 'k_sub')
                                    for kb in range(qt + 1):
                                        bk = BK(kb)
                                        b.wr(tt, 'tensor', f'b{bk}')
                                        tt.matmul(
                                            ps[bk][:, 0:512],
                                            qn_grp[:, h4 * 1024 + qt * 128: h4 * 1024 + (qt + 1) * 128],
                                            k_sub[:, j2 * S + kb * 512: j2 * S + (kb + 1) * 512],
                                            start=True, stop=False)
                                        ins = tt.matmul(
                                            ps[bk][:, 0:512],
                                            p['peT'][p['pb']:p['pb'] + 64, qt * 128:(qt + 1) * 128],
                                            k_rot[p['pb']:p['pb'] + 64, kb * 512:(kb + 1) * 512],
                                            start=False, stop=(kb != qt),
                                            tile_position=(p['pb'], 0))
                                        if kb == qt:
                                            ins = tt.matmul(
                                                ps[bk][:, 0:512], c_ident[:, :], c_mask[:, :],
                                                start=False, stop=True)
                                        ev = b.inc(ins, 'pe')
                                        b.note_w(f'b{bk}', ev)
                                    b.note_r('qn_grp', ev)
                                    b.note_r(p['peName'], ev)
                                    b.note_r('k_sub', ev)

                            def emit_max(p, qt):
                                nkb = qt + 1
                                @blk.vector
                                def _(v, p=p, qt=qt, nkb=nkb):
                                    j2, BK = p['j2'], p['BK']
                                    if nkb < 8:
                                        v.memset(mxb[j2][:, nkb:8], -1e30)
                                    for kb in range(nkb):
                                        bk = BK(kb)
                                        b.rd(v, 'vector', f'b{bk}')
                                        v.tensor_reduce(mxb[j2][:, kb:kb + 1], ps[bk][:, 0:512],
                                                        AX.X, ALU.max)
                                    v.tensor_tensor(tred[j2][:, 14:15], biasv[j2][:, :], biasv[j2][:, :], ALU.add)
                                    v.tensor_tensor(tred[j2][:, 14:15], biasv[j2][:, :], biasv[j2][:, :], ALU.add)
                                    v.tensor_tensor(tred[j2][:, 14:15], biasv[j2][:, :], biasv[j2][:, :], ALU.add)
                                    v.tensor_tensor(tred[j2][:, 0:4], mxb[j2][:, 0:4], mxb[j2][:, 4:8], ALU.max)
                                    v.tensor_tensor(tred[j2][:, 4:6], tred[j2][:, 0:2], tred[j2][:, 2:4], ALU.max)
                                    v.tensor_tensor(tred[j2][:, 6:7], tred[j2][:, 4:5], tred[j2][:, 5:6], ALU.max)
                                    ev = b.inc(v.tensor_scalar(biasv[j2][:, :], tred[j2][:, 6:7],
                                                               float(-SCALE), None, ALU.mult), 'dve')
                                    b.note_w(f'biasv{j2}', ev)
                                    for kb in range(nkb):
                                        b.note_r(f'b{BK(kb)}', ev)

                            def emit_exp(p, qt):
                                @blk.scalar
                                def _(s, p=p, qt=qt):
                                    j2, BK = p['j2'], p['BK']
                                    b.rd(s, 'scalar', f'biasv{j2}')
                                    for kb in range(qt + 1):
                                        bk = BK(kb)
                                        b.wr(s, 'scalar', f'pu{j2}_{kb}')
                                        ev = b.inc(s.activation(
                                            probs_u[j2][:, kb * 512:(kb + 1) * 512],
                                            ps[bk][:, 0:512], AF.Exp,
                                            bias=biasv[j2][:, :], scale=float(SCALE)), 'act')
                                        b.note_w(f'pu{j2}_{kb}', ev)
                                        b.note_r(f'b{bk}', ev)

                            def emit_transp(p, qt):
                                @blk.tensor
                                def _(tt, p=p, qt=qt):
                                    j2, BK = p['j2'], p['BK']
                                    for kb in range(qt + 1):
                                        bk = BK(kb)
                                        b.rd(tt, 'tensor', f'pu{j2}_{kb}')
                                        b.wr(tt, 'tensor', f'b{bk}')
                                        ins = None
                                        for ss in range(4):
                                            ins = tt.transpose(
                                                psb[bk][:, ss * 128:(ss + 1) * 128],
                                                probs_u[j2][:, kb * 512 + ss * 128: kb * 512 + (ss + 1) * 128],
                                                c_ident[:, :])
                                        ev = b.inc(ins, 'pe')
                                        b.note_w(f'b{bk}', ev)
                                        b.note_r(f'pu{j2}_{kb}', ev)

                            def emit_evac(p, qt):
                                @blk.vector
                                def _(v, p=p, qt=qt):
                                    j2, BK = p['j2'], p['BK']
                                    for kb in range(qt + 1):
                                        bk = BK(kb)
                                        b.rd(v, 'vector', f'b{bk}')
                                        b.wr(v, 'vector', f'pT{j2}_{kb}')
                                        ev = b.inc(v.tensor_copy(
                                            probsT[j2][:, kb * 512:(kb + 1) * 512],
                                            psb[bk][:, 0:512]), 'dve')
                                        b.note_r(f'b{bk}', ev)
                                        b.note_w(f'pT{j2}_{kb}', ev)

                            def emit_pv(p, qt):
                                @blk.tensor
                                def _(tt, p=p, qt=qt):
                                    j2, pvb = p['j2'], p['pvb']
                                    b.wr(tt, 'tensor', f'pv{j2}')
                                    b.wr(tt, 'tensor', f'b{pvb}')
                                    nsc = 4 * (qt + 1)
                                    ins = None
                                    for sc in range(nsc):
                                        b.rd(tt, 'tensor', f'pT{j2}_{sc // 4}')
                                        ins = tt.matmul(
                                            ps[pvb][:, 383:512],
                                            probsT[j2][:, sc * 128:(sc + 1) * 128],
                                            v_sub[:, sc * 258 + j2 * 129: sc * 258 + (j2 + 1) * 129],
                                            start=(sc == 0), stop=(sc == nsc - 1))
                                    ev = b.inc(ins, 'pe')
                                    b.note_w(f'pv{j2}', ev)
                                    b.note_r(f'b{pvb}', ev)
                                    b.note_r('v_sub', ev)
                                    for kb in range(qt + 1):
                                        b.note_r(f'pT{j2}_{kb}', ev)

                                @blk.vector
                                def _(v, p=p, qt=qt):
                                    j2, pvb = p['j2'], p['pvb']
                                    b.rd(v, 'vector', f'pv{j2}')
                                    b.wr(v, 'vector', f'outst{j2}_{qt}')
                                    v.reciprocal(recipb[j2][:, :], ps[pvb][:, 511:512])
                                    v.tensor_tensor(tred[j2][:, 14:15], biasv[j2][:, :], biasv[j2][:, :], ALU.add)
                                    v.tensor_tensor(tred[j2][:, 14:15], biasv[j2][:, :], biasv[j2][:, :], ALU.add)
                                    ev = b.inc(v.tensor_scalar(
                                        outst[j2][:, qt * 128:(qt + 1) * 128],
                                        ps[pvb][:, 383:511], recipb[j2][:, :], None, ALU.mult), 'dve')
                                    b.note_r(f'pv{j2}', ev)
                                    b.note_r(f'b{pvb}', ev)
                                    b.note_w(f'outst{j2}_{qt}', ev)

                            schedule = [[(0, i), (1, 6 - i)] for i in range(7)] + [[(0, 7)], [(1, 7)]]
                            for step in schedule:
                                for (jj, qt) in step:
                                    emit_scores(P[jj], qt)
                                for (jj, qt) in step:
                                    emit_max(P[jj], qt)
                                for (jj, qt) in step:
                                    emit_exp(P[jj], qt)
                                for (jj, qt) in step:
                                    emit_transp(P[jj], qt)
                                for (jj, qt) in step:
                                    emit_evac(P[jj], qt)
                                for (jj, qt) in step:
                                    emit_pv(P[jj], qt)

                            for p in P:
                                @blk.tensor
                                def _(tt, p=p):
                                    j2, hdb = p['j2'], p['hdb']
                                    for qt2 in range(NQT):
                                        b.rd(tt, 'tensor', f'outst{j2}_{qt2}')
                                    b.wr(tt, 'tensor', f'b{hdb}')
                                    ins = None
                                    for qt2 in range(NQT):
                                        ins = tt.transpose(psb[hdb][:, qt2 * 128:(qt2 + 1) * 128],
                                                           outst[j2][:, qt2 * 128:(qt2 + 1) * 128],
                                                           c_ident[:, :])
                                    ev = b.inc(ins, 'pe')
                                    b.note_w(f'b{hdb}', ev)
                                    for qt2 in range(NQT):
                                        b.note_r(f'outst{j2}_{qt2}', ev)

                                @blk.vector
                                def _(v, p=p):
                                    j2, hdb, apar = p['j2'], p['hdb'], p['apar']
                                    b.rd(v, 'vector', f'b{hdb}')
                                    b.wr(v, 'vector', f'arow{apar}')
                                    ev = b.inc(v.tensor_copy(arow[apar][:, :], psb[hdb][:, 0:1024]), 'dve')
                                    b.note_r(f'b{hdb}', ev)
                                    b.note_w(f'arow{apar}', ev)

                                @blk.sync
                                def _(sy, p=p):
                                    apar = p['apar']
                                    b.rd(sy, 'sync', f'arow{apar}')
                                    ev = b.inc(sy.dma_start(d_attn.ap()[p['hh']], arow[apar][:, :]),
                                               f'ar{apar}', 16)
                                    b.note_r(f'arow{apar}', ev)

                            if 'pu0' in d_dbg and g4 == 0 and sub == 0:
                                @blk.sync
                                def _(sy):
                                    b.wait(sy, 'sync', b.tot('act'))
                                    b.inc(sy.dma_start(d_dbg['pu0'].ap(), probs_u[0][:, :]), 'dbg', 16)
                                    sy.wait_ge(b.sem('dbg'), b.cnt['dbg'])

                            if 'k0' in d_dbg and g4 == 0 and sub == 0:
                                @blk.sync
                                def _(sy):
                                    b.rd(sy, 'sync', 'k_sub')
                                    b.wait(sy, 'sync', b.tot('act'))
                                    b.inc(sy.dma_start(d_dbg['k0'].ap(), k_sub[:, :]), 'dbg', 16)
                                    sy.wait_ge(b.sem('dbg'), b.cnt['dbg'])

                            if 'v0' in d_dbg and g4 == 0 and sub == 0:
                                @blk.sync
                                def _(sy):
                                    b.wait(sy, 'sync', b.tot('dve'))
                                    b.inc(sy.dma_start(d_dbg['v0'].ap(), v_sub[:, :]), 'dbg', 16)
                                    sy.wait_ge(b.sem('dbg'), b.cnt['dbg'])

                        if 'qn0' in d_dbg and g4 == 0:
                            @blk.sync
                            def _(sy):
                                b.wait(sy, 'sync', b.tot('dve'))
                                b.inc(sy.dma_start(d_dbg['qn0'].ap(), qn_grp[:, :]), 'dbg', 16)
                                sy.wait_ge(b.sem('dbg'), b.cnt['dbg'])

                        if 'pe0' in d_dbg and g4 == 0:
                            @blk.sync
                            def _(sy):
                                b.wait(sy, 'sync', b.tot('pem'))
                                b.inc(sy.dma_start(d_dbg['pe0'].ap()[:, 0:1024], peA[:, :]), 'dbg', 16)
                                b.inc(sy.dma_start(d_dbg['pe0'].ap()[:, 1024:2048], peB[:, :]), 'dbg', 16)
                                sy.wait_ge(b.sem('dbg'), b.cnt['dbg'])
                b.barrier()

        es_m = contextlib.ExitStack()
        with es_m:
            zbuf = es_m.enter_context(nc.sbuf_tensor("zbuf", [128, NOC * 1024], BF16))
            m_sum = es_m.enter_context(nc.sbuf_tensor("m_sum", [1, 1024], F32))
            m_ssq = es_m.enter_context(nc.sbuf_tensor("m_ssq", [1, 1024], F32))
            m_mean = es_m.enter_context(nc.sbuf_tensor("m_mean", [1, 1024], F32))
            m_rstd = es_m.enter_context(nc.sbuf_tensor("m_rstd", [1, 1024], F32))
            m_nmr = es_m.enter_context(nc.sbuf_tensor("m_nmr", [1, 1024], F32))
            sqb2 = [es_m.enter_context(nc.sbuf_tensor(f"sqb2{i}", [128, 1024], BF16)) for i in range(2)]
            tmpf = es_m.enter_context(nc.sbuf_tensor("tmpf", [128, 1024], F32))

            def ln_pass(blk, out_writer):
                for oc in range(NOC):
                    spar = oc % 2

                    @blk.scalar
                    def _(s, oc=oc, spar=spar):
                        b.wr(s, 'scalar', f'sqb2{spar}')
                        sl = zbuf[:, oc * 1024:(oc + 1) * 1024]
                        ev = b.inc(s.activation(sqb2[spar][:, :], sl, AF.Square), 'act')
                        b.note_w(f'sqb2{spar}', ev)

                    @blk.tensor
                    def _(tt, oc=oc, spar=spar):
                        b.rd(tt, 'tensor', f'sqb2{spar}')
                        if oc == 0:
                            for bank in range(4):
                                b.wr(tt, 'tensor', f'b{bank}')
                        ins = None
                        for qb in range(2):
                            tt.matmul(ps[0 + qb][0:1, 0:512], c_onesb[:, 0:1],
                                      zbuf[:, oc * 1024 + qb * 512: oc * 1024 + (qb + 1) * 512],
                                      start=(oc == 0), stop=(oc == NOC - 1))
                            ins = tt.matmul(ps[2 + qb][0:1, 0:512], c_onesb[:, 0:1],
                                            sqb2[spar][:, qb * 512:(qb + 1) * 512],
                                            start=(oc == 0), stop=(oc == NOC - 1))
                        ev = b.inc(ins, 'pe')
                        b.note_r(f'sqb2{spar}', ev)
                        if oc == NOC - 1:
                            for bank in range(4):
                                b.note_w(f'b{bank}', ev)

                @blk.vector
                def _(v):
                    for bank in range(4):
                        b.rd(v, 'vector', f'b{bank}')
                    v.tensor_copy(m_sum[0:1, 0:512], ps[0][0:1, 0:512])
                    v.tensor_copy(m_sum[0:1, 512:1024], ps[1][0:1, 0:512])
                    v.tensor_copy(m_ssq[0:1, 0:512], ps[2][0:1, 0:512])
                    v.tensor_copy(m_ssq[0:1, 512:1024], ps[3][0:1, 0:512])
                    v.tensor_scalar(m_mean[:, :], m_sum[:, :], float(1.0 / H), None, ALU.mult)
                    v.tensor_scalar(m_ssq[:, :], m_ssq[:, :], float(1.0 / H), None, ALU.mult)
                    v.tensor_tensor(m_rstd[:, :], m_mean[:, :], m_mean[:, :], ALU.mult)
                    ev = b.inc(v.tensor_tensor(m_rstd[:, :], m_ssq[:, :], m_rstd[:, :], ALU.subtract), 'dve')
                    b.note_w('m_var', ev)
                    for bank in range(4):
                        b.note_r(f'b{bank}', ev)

                @blk.scalar
                def _(s):
                    b.rd(s, 'scalar', 'm_var')
                    ev = b.inc(s.activation(m_rstd[:, :], m_rstd[:, :], AF.Sqrt,
                                            bias=c_eps[0:1, 1:2], scale=1.0), 'act')
                    b.note_w('m_var', ev)

                @blk.vector
                def _(v):
                    b.rd(v, 'vector', 'm_var')
                    v.reciprocal(m_rstd[:, :], m_rstd[:, :])
                    v.tensor_tensor(m_ssq[:, :], m_mean[:, :], m_mean[:, :], ALU.add)
                    v.tensor_tensor(m_ssq[:, :], m_mean[:, :], m_mean[:, :], ALU.add)
                    v.tensor_tensor(m_ssq[:, :], m_mean[:, :], m_mean[:, :], ALU.add)
                    v.tensor_tensor(m_nmr[:, :], m_mean[:, :], m_rstd[:, :], ALU.mult)
                    ev = b.inc(v.tensor_scalar(m_nmr[:, :], m_nmr[:, :], -1.0, None, ALU.mult), 'dve')
                    b.note_w('m_fin', ev)

                for qb in range(2):
                    @blk.tensor
                    def _(tt, qb=qb):
                        b.rd(tt, 'tensor', 'm_fin')
                        b.wr(tt, 'tensor', f'b{4 + qb}')
                        b.wr(tt, 'tensor', f'b{6 + qb}')
                        ev = b.inc(tt.matmul(ps[4 + qb][:, 0:512], c_ones1[0:1, :],
                                             m_rstd[0:1, qb * 512:(qb + 1) * 512],
                                             start=True, stop=True), 'pe')
                        b.note_w(f'b{4 + qb}', ev)
                        ev = b.inc(tt.matmul(ps[6 + qb][:, 0:512], c_ones1[0:1, :],
                                             m_nmr[0:1, qb * 512:(qb + 1) * 512],
                                             start=True, stop=True), 'pe')
                        b.note_w(f'b{6 + qb}', ev)

                for oc in range(NOC):
                    for qb in range(2):
                        @blk.vector
                        def _(v, oc=oc, qb=qb):
                            b.rd(v, 'vector', f'b{4 + qb}')
                            b.rd(v, 'vector', f'b{6 + qb}')
                            out_writer(v, oc, qb, ps[4 + qb][:, 0:512], ps[6 + qb][:, 0:512])

            with contextlib.ExitStack() as es4:
                attnT = es4.enter_context(nc.sbuf_tensor("attnT", [128, NH * 1024], BF16))
                wob = [es4.enter_context(nc.sbuf_tensor(f"wob{i}", [128, 32 * 128], BF16)) for i in range(2)]
                xqf = [es4.enter_context(nc.sbuf_tensor(f"xqf{i}", [128, 1024], F32)) for i in range(2)]

                with nc.Block() as blk:
                    @blk.sync
                    def _(sy):
                        sy.wait_ge(b.sem('ar0'), b.cnt['ar0'])
                        sy.wait_ge(b.sem('ar1'), b.cnt['ar1'])
                        for h in range(0, NH, 2):
                            b.inc(sy.dma_start(attnT[:, h * 1024:(h + 1) * 1024],
                                               d_attn.ap()[h]), 'at', 16)

                    @blk.scalar
                    def _(s):
                        s.wait_ge(b.sem('ar0'), b.cnt['ar0'])
                        s.wait_ge(b.sem('ar1'), b.cnt['ar1'])
                        for h in range(1, NH, 2):
                            b.inc(s.dma_start(attnT[:, h * 1024:(h + 1) * 1024],
                                              d_attn.ap()[h]), 'at', 16)
                        b.note_w('attnT', ('at', b.cnt['at']))

                    for oc in range(NOC):
                        par = oc % 2

                        @blk.gpsimd
                        def _(g, oc=oc, par=par):
                            b.wr(g, 'gpsimd', f'wob{par}')
                            ev = b.inc(g.dma_start(wob[par][:, :],
                                                   d_wo.ap()[oc].rearrange("p c m -> p (c m)"),
                                                   max_dma_last_dim=2048), f'wo{par}', 16)
                            b.note_w(f'wob{par}', ev)

                        @blk.sync
                        def _(sy, oc=oc, par=par):
                            b.wr(sy, 'sync', f'xqf{par}')
                            ev = b.inc(sy.dma_start(xqf[par][:, :], d_xq.ap()[:, oc, :]), f'xr{par}', 16)
                            b.note_w(f'xqf{par}', ev)

                        for qb in range(2):
                            bank = (oc * 2 + qb) % 6

                            @blk.tensor
                            def _(tt, oc=oc, qb=qb, bank=bank, par=par):
                                b.rd(tt, 'tensor', 'attnT')
                                b.rd(tt, 'tensor', f'wob{par}')
                                b.wr(tt, 'tensor', f'b{bank}')
                                ins = None
                                for jc in range(NH):
                                    ins = tt.matmul(
                                        ps[bank][:, 0:512],
                                        wob[par][:, jc * 128:(jc + 1) * 128],
                                        attnT[:, jc * 1024 + qb * 512: jc * 1024 + (qb + 1) * 512],
                                        start=(jc == 0), stop=(jc == NH - 1))
                                ev = b.inc(ins, 'pe')
                                b.note_w(f'b{bank}', ev)
                                if qb == 1:
                                    b.note_r(f'wob{par}', ev)

                            @blk.vector
                            def _(v, oc=oc, qb=qb, bank=bank, par=par):
                                b.rd(v, 'vector', f'b{bank}')
                                b.rd(v, 'vector', f'xqf{par}')
                                ev = b.inc(v.tensor_tensor(
                                    zbuf[:, oc * 1024 + qb * 512: oc * 1024 + (qb + 1) * 512],
                                    ps[bank][:, 0:512], xqf[par][:, qb * 512:(qb + 1) * 512],
                                    ALU.add), 'dve')
                                b.note_r(f'b{bank}', ev)
                                if qb == 1:
                                    b.note_r(f'xqf{par}', ev)

                    def w1(v, oc, qb, rb, nb):
                        sl = zbuf[:, oc * 1024 + qb * 512: oc * 1024 + (qb + 1) * 512]
                        v.tensor_tensor(tmpf[:, 0:512], sl, rb, ALU.mult)
                        ev = b.inc(v.tensor_tensor(sl, tmpf[:, 0:512], nb, ALU.add), 'dve')
                        if oc == NOC - 1 and qb == 1:
                            for bank in range(4, 8):
                                b.note_r(f'b{bank}', ev)

                    ln_pass(blk, w1)
                b.barrier()

            dump('h1n', zbuf[:, :])

            with contextlib.ExitStack() as es5:
                gub = [es5.enter_context(nc.sbuf_tensor(f"gub{i}", [128, NKC * 128], BF16)) for i in range(2)]
                uub = [es5.enter_context(nc.sbuf_tensor(f"uub{i}", [128, NKC * 128], BF16)) for i in range(2)]
                gact = es5.enter_context(nc.sbuf_tensor("gact", [128, 1024], BF16))
                prst = [es5.enter_context(nc.sbuf_tensor(f"prst{i}", [128, 1024], BF16)) for i in range(2)]

                with nc.Block() as blk:
                    for ic in range(NIC):
                        par = ic % 2

                        @blk.gpsimd
                        def _(g, ic=ic, par=par):
                            b.wr(g, 'gpsimd', f'gub{par}')
                            ev = b.inc(g.dma_start(gub[par][:, :],
                                                   d_gw.ap()[ic].rearrange("p c m -> p (c m)"),
                                                   max_dma_last_dim=2048), f'gl{par}', 16)
                            b.note_w(f'gub{par}', ev)
                            b.wr(g, 'gpsimd', f'uub{par}')
                            ev = b.inc(g.dma_start(uub[par][:, :],
                                                   d_uw.ap()[ic].rearrange("p c m -> p (c m)"),
                                                   max_dma_last_dim=2048), f'ul{par}', 16)
                            b.note_w(f'uub{par}', ev)

                        for qb in range(2):
                            gbank = 4 * par + qb
                            ubank = 4 * par + 2 + qb

                            @blk.tensor
                            def _(tt, ic=ic, qb=qb, par=par, gbank=gbank, ubank=ubank):
                                b.rd(tt, 'tensor', f'gub{par}')
                                b.rd(tt, 'tensor', f'uub{par}')
                                b.wr(tt, 'tensor', f'b{gbank}')
                                b.wr(tt, 'tensor', f'b{ubank}')
                                for c in range(NKC):
                                    tt.matmul(ps[gbank][:, 0:512],
                                              gub[par][:, c * 128:(c + 1) * 128],
                                              zbuf[:, c * 1024 + qb * 512: c * 1024 + (qb + 1) * 512],
                                              start=(c == 0), stop=(c == NKC - 1))
                                ins = None
                                for c in range(NKC):
                                    ins = tt.matmul(ps[ubank][:, 0:512],
                                                    uub[par][:, c * 128:(c + 1) * 128],
                                                    zbuf[:, c * 1024 + qb * 512: c * 1024 + (qb + 1) * 512],
                                                    start=(c == 0), stop=(c == NKC - 1))
                                ev = b.inc(ins, 'pe')
                                b.note_w(f'b{gbank}', ev)
                                b.note_w(f'b{ubank}', ev)
                                if qb == 1:
                                    b.note_r(f'gub{par}', ev)
                                    b.note_r(f'uub{par}', ev)

                            @blk.scalar
                            def _(s, ic=ic, qb=qb, gbank=gbank):
                                b.rd(s, 'scalar', f'b{gbank}')
                                b.wr(s, 'scalar', f'gact{qb}')
                                ev = b.inc(s.activation(gact[:, qb * 512:(qb + 1) * 512],
                                                        ps[gbank][:, 0:512], AF.Silu,
                                                        scale=c_gs[:, :]), 'act')
                                b.note_w(f'gact{qb}', ev)
                                b.note_r(f'b{gbank}', ev)

                            @blk.vector
                            def _(v, ic=ic, qb=qb, par=par, ubank=ubank):
                                b.rd(v, 'vector', f'b{ubank}')
                                b.rd(v, 'vector', f'gact{qb}')
                                b.wr(v, 'vector', f'prst{par}')
                                ev = b.inc(v.tensor_tensor(
                                    prst[par][:, qb * 512:(qb + 1) * 512],
                                    ps[ubank][:, 0:512], gact[:, qb * 512:(qb + 1) * 512],
                                    ALU.mult), 'dve')
                                b.note_r(f'b{ubank}', ev)
                                b.note_r(f'gact{qb}', ev)
                                if qb == 1:
                                    b.note_w(f'prst{par}', ev)

                        @blk.sync
                        def _(sy, ic=ic, par=par):
                            b.rd(sy, 'sync', f'prst{par}')
                            ev = b.inc(sy.dma_start(d_prod.ap()[:, ic, :], prst[par][:, :]),
                                       f'pw{par}', 16)
                            b.note_r(f'prst{par}', ev)
                b.barrier()

            with contextlib.ExitStack() as es6:
                dwb = [es6.enter_context(nc.sbuf_tensor(f"dwb{i}", [128, 4 * 512], BF16)) for i in range(2)]
                prb = [es6.enter_context(nc.sbuf_tensor(f"prb{i}", [128, 4 * 1024], BF16)) for i in range(2)]

                ic_chunks = [(s0, min(s0 + 4, NIC)) for s0 in range(0, NIC, 4)]
                with nc.Block() as blk:
                    @blk.sync
                    def _(sy):
                        sy.wait_ge(b.sem('pw0'), b.cnt['pw0'])
                        sy.wait_ge(b.sem('pw1'), b.cnt['pw1'])

                    for og in range(8):
                        for ci, (s0, s1) in enumerate(ic_chunks):
                            par = ci % 2

                            @blk.gpsimd
                            def _(g, og=og, s0=s0, s1=s1, par=par):
                                b.wr(g, 'gpsimd', f'dwb{par}')
                                ev = b.inc(g.dma_start(
                                    dwb[par][:, 0:(s1 - s0) * 512],
                                    d_dw.ap()[og][:, s0:s1, :].rearrange("p c m -> p (c m)"),
                                    max_dma_last_dim=2048), f'dl{par}', 16)
                                b.note_w(f'dwb{par}', ev)

                            @blk.sync
                            def _(sy, og=og, s0=s0, s1=s1, par=par):
                                b.wr(sy, 'sync', f'prb{par}')
                                ev = b.inc(sy.dma_start(
                                    prb[par][:, 0:(s1 - s0) * 1024],
                                    d_prod.ap()[:, s0:s1, :].rearrange("p c m -> p (c m)")),
                                    f'pr{par}', 16)
                                b.note_w(f'prb{par}', ev)

                            @blk.tensor
                            def _(tt, og=og, ci=ci, s0=s0, s1=s1, par=par):
                                b.rd(tt, 'tensor', f'dwb{par}')
                                b.rd(tt, 'tensor', f'prb{par}')
                                if ci == 0:
                                    for bank in range(8):
                                        b.wr(tt, 'tensor', f'b{bank}')
                                ins = None
                                for io in range(s1 - s0):
                                    icg = s0 + io
                                    for oc in range(4):
                                        for qb in range(2):
                                            ins = tt.matmul(
                                                ps[oc * 2 + qb][:, 0:512],
                                                dwb[par][:, io * 512 + oc * 128: io * 512 + (oc + 1) * 128],
                                                prb[par][:, io * 1024 + qb * 512: io * 1024 + (qb + 1) * 512],
                                                start=(icg == 0), stop=(icg == NIC - 1))
                                ev = b.inc(ins, 'pe')
                                b.note_r(f'dwb{par}', ev)
                                b.note_r(f'prb{par}', ev)
                                if ci == len(ic_chunks) - 1:
                                    for bank in range(8):
                                        b.note_w(f'b{bank}', ev)

                        for oc in range(4):
                            for qb in range(2):
                                @blk.vector
                                def _(v, og=og, oc=oc, qb=qb):
                                    bank = oc * 2 + qb
                                    b.rd(v, 'vector', f'b{bank}')
                                    occ = og * 4 + oc
                                    sl = zbuf[:, occ * 1024 + qb * 512: occ * 1024 + (qb + 1) * 512]
                                    v.tensor_scalar(tmpf[:, 0:512], ps[bank][:, 0:512],
                                                    c_us[:, :], c_ds[:, :], ALU.mult, ALU.mult)
                                    ev = b.inc(v.tensor_tensor(sl, tmpf[:, 0:512], sl, ALU.add), 'dve')
                                    b.note_r(f'b{bank}', ev)
                b.barrier()

            dump('z2', zbuf[:, :])

            with contextlib.ExitStack() as es7:
                outf = [es7.enter_context(nc.sbuf_tensor(f"outf{i}", [128, 1024], F32)) for i in range(4)]

                with nc.Block() as blk:
                    def w2(v, oc, qb, rb, nb):
                        par = oc % 4
                        sl = zbuf[:, oc * 1024 + qb * 512: oc * 1024 + (qb + 1) * 512]
                        if qb == 0:
                            b.wr(v, 'vector', f'outf{par}')
                        v.tensor_tensor(tmpf[:, 0:512], sl, rb, ALU.mult)
                        ev = b.inc(v.tensor_tensor(outf[par][:, qb * 512:(qb + 1) * 512],
                                                   tmpf[:, 0:512], nb, ALU.add), 'dve')
                        if qb == 1:
                            b.note_w(f'outf{par}', ev)
                            b.note_w(f'outev{oc}', ev)
                        if oc == NOC - 1 and qb == 1:
                            for bank in range(4, 8):
                                b.note_r(f'b{bank}', ev)

                    ln_pass(blk, w2)

                    for oc in range(NOC):
                        par = oc % 4
                        eng = 'sync' if oc % 2 == 0 else 'scalar'

                        if oc % 2 == 0:
                            @blk.sync
                            def _(sy, oc=oc, par=par):
                                b.rd(sy, 'sync', f'outev{oc}')
                                ev = b.inc(sy.dma_start(d_out.ap()[oc], outf[par][:, :]), f'os{par}', 16)
                                b.note_r(f'outf{par}', ev)
                        else:
                            @blk.scalar
                            def _(s, oc=oc, par=par):
                                b.rd(s, 'scalar', f'outev{oc}')
                                ev = b.inc(s.dma_start(d_out.ap()[oc], outf[par][:, :]), f'os{par}', 16)
                                b.note_r(f'outf{par}', ev)

                    @blk.sync
                    def _(sy):
                        for p4 in range(4):
                            sy.wait_ge(b.sem(f'os{p4}'), b.cnt[f'os{p4}'])
                b.barrier()
    return nc



def _pack_lhsT(w, gm):
    Mo, K = w.shape
    G = Mo // gm
    nk = K // 128
    a = w.reshape(G, gm, nk, 128).transpose(0, 3, 2, 1)
    return np.ascontiguousarray(a)


def _rope_tables():
    freqs = (1.0 / 10000.0 ** (np.arange(0, DR, 2, dtype=np.float32) / DR)).astype(np.float32)
    t = np.arange(S, dtype=np.float32)
    ang = np.outer(t, freqs).astype(np.float32)
    return np.cos(ang).astype(np.float32), np.sin(ang).astype(np.float32)


def prep_inputs(inputs):
    x = np.asarray(inputs['L0_attn_x'], dtype=np.float32)
    wq_a = np.asarray(inputs['L0_attn_wq_a'], dtype=np.float32)
    q_norm_w = np.asarray(inputs['L0_attn_q_norm_w'], dtype=np.float32)
    wq_b = np.asarray(inputs['L0_attn_wq_b'], dtype=np.float32)
    wkv_a = np.asarray(inputs['L0_attn_wkv_a'], dtype=np.float32)
    kv_norm_w = np.asarray(inputs['L0_attn_kv_norm_w'], dtype=np.float32)
    wkv_b = np.asarray(inputs['L0_attn_wkv_b'], dtype=np.float32)
    wo = np.asarray(inputs['L0_attn_wo'], dtype=np.float32)
    gate_w = np.asarray(inputs['L0_ffn_gate_w'], dtype=np.int32)
    gate_s = np.asarray(inputs['L0_ffn_gate_s'], dtype=np.float32)
    up_w = np.asarray(inputs['L0_ffn_up_w'], dtype=np.int32)
    up_s = np.asarray(inputs['L0_ffn_up_s'], dtype=np.float32)
    down_w = np.asarray(inputs['L0_ffn_down_w'], dtype=np.int32)
    down_s = np.asarray(inputs['L0_ffn_down_s'], dtype=np.float32)

    cos_t, sin_t = _rope_tables()
    bf = ml_dtypes.bfloat16

    perm_kva = np.concatenate([np.arange(KVL), KVL + np.arange(0, DR, 2), KVL + np.arange(1, DR, 2)])
    wkva_p = _pack_lhsT(wkv_a[perm_kva], 576)[0]

    rows = []
    for g in range(8):
        for h in range(4 * g, 4 * g + 4):
            rows.append(h * 192 + np.arange(DN))
        for h in range(4 * g, 4 * g + 4):
            rows.append(h * 192 + DN + np.arange(0, DR, 2))
        for h in range(4 * g, 4 * g + 4):
            rows.append(h * 192 + DN + np.arange(1, DR, 2))
    perm_qb = np.concatenate(rows)
    wqb_p = _pack_lhsT(wq_b[perm_qb], 768)

    rows_k = np.concatenate([h * 256 + np.arange(DN) for h in range(NH)])
    rows_v = np.concatenate([h * 256 + DN + np.arange(DV) for h in range(NH)])
    wkbk_p = _pack_lhsT(wkv_b[rows_k], 512)
    wkbv_p = _pack_lhsT(wkv_b[rows_v], 512)

    wqa_p = _pack_lhsT(wq_a, 256)
    wo_p = _pack_lhsT(wo, 128)
    gw_p = _pack_lhsT(gate_w, 128)
    uw_p = _pack_lhsT(up_w, 128)
    dw_p = _pack_lhsT(down_w, 512)

    shared = dict(
        cosk=np.ascontiguousarray(cos_t.T), sink=np.ascontiguousarray(sin_t.T),
        ident=np.eye(128, dtype=np.float32).astype(bf),
        ones1=np.ones((1, 128), np.float32),
        onesb=np.ones((128, 1), np.float32).astype(bf),
        qnw=np.ascontiguousarray(q_norm_w.reshape(12, 128).T),
        kvnw=np.ascontiguousarray(kv_norm_w.reshape(4, 128).T),
        epsv=np.array([[EPS_RMS, EPS_LN]], dtype=np.float32),
        gs=np.ascontiguousarray(np.broadcast_to(gate_s.reshape(1, 1), (128, 1))),
        us=np.ascontiguousarray(np.broadcast_to(up_s.reshape(1, 1), (128, 1))),
        ds=np.ascontiguousarray(np.broadcast_to(down_s.reshape(1, 1), (128, 1))),
        wkva=wkva_p, wqa=wqa_p, wqb=wqb_p, wkbk=wkbk_p, wkbv=wkbv_p,
        wo=wo_p, gw=gw_p, uw=uw_p, dw=dw_p,
    )

    xp_b = [np.ascontiguousarray(x[bb].reshape(16, 256, NKC, 128).transpose(0, 3, 2, 1))
            for bb in range(B)]

    per_core = []
    for c in range(8):
        bb, qq = c // 4, c % 4
        qpos = (128 * (4 * np.arange(8)[:, None] + qq) + np.arange(128)[None, :]).reshape(-1)
        xg = x[bb][qpos]
        xq_p = np.ascontiguousarray(xg.reshape(1024, NKC, 128).transpose(2, 1, 0))
        cosq = np.ascontiguousarray(np.tile(cos_t[qpos].T, (4, 1)))
        sinq = np.ascontiguousarray(np.tile(sin_t[qpos].T, (4, 1)))
        gp = 128 * qq + np.arange(128)[:, None]
        kp = np.arange(512)[None, :]
        mask = np.where(kp <= gp, 0.0, NEG).astype(np.float32)
        per_core.append(dict(xp=xp_b[bb], xq=xq_p, cosq=cosq, sinq=sinq,
                             mask_bf=mask.astype(bf), qpos=qpos))
    return shared, per_core


def kernel(**inputs):
    shared, per_core = prep_inputs(inputs)
    nc = build_nc()
    in_maps = []
    for c in range(8):
        m = dict(shared)
        m['xp'] = per_core[c]['xp']
        m['xq'] = per_core[c]['xq']
        m['cosq'] = per_core[c]['cosq']
        m['sinq'] = per_core[c]['sinq']
        m['maskd'] = per_core[c]['mask_bf']
        in_maps.append(m)
    res = run_bass_kernel_spmd(nc, in_maps, core_ids=list(range(8)))

    out = np.zeros((B, S, H), dtype=np.float32)
    for c in range(8):
        bb = c // 4
        qpos = per_core[c]['qpos']
        o = np.asarray(res.results[c]['out'])
        o = o.transpose(2, 0, 1).reshape(1024, H)
        out[bb][qpos] = o
    return out
